# version 02
# baseline (speedup 1.0000x reference)
import contextlib
import numpy as np
import concourse.bass as bass
import concourse.mybir as mybir
from concourse.bass_utils import run_bass_kernel_spmd

F32 = mybir.dt.float32
BF16 = mybir.dt.bfloat16
AF = mybir.ActivationFunctionType
ALU = mybir.AluOpType
AX = mybir.AxisListType

NCORES = 8
DM = 4096
NBLK = 32
TLOC = 2048
TC = 512
NCH = TLOC // TC
NST = 16
GR = 2
NG = NST // GR
EPS = 1e-5


def build_nc(dbg=False):
    nc = bass.Bass("TRN2", target_bir_lowering=False, debug=False)

    x_h = nc.dram_tensor("x", [DM, TLOC], F32, kind="ExternalInput").ap()
    wxp_h = nc.dram_tensor("wxp", [DM, 160], F32, kind="ExternalInput").ap()
    wdt_h = nc.dram_tensor("wdt", [128, DM], F32, kind="ExternalInput").ap()
    alog_h = nc.dram_tensor("alog", [128, NBLK * NST], F32, kind="ExternalInput").ap()
    dvec_h = nc.dram_tensor("dvec", [128, NBLK], F32, kind="ExternalInput").ap()
    bvec_h = nc.dram_tensor("bvec", [128, NBLK], F32, kind="ExternalInput").ap()
    route_h = nc.dram_tensor("route", [DM, 8], F32, kind="ExternalInput").ap()
    affn_h = nc.dram_tensor("affn", [DM, 128], F32, kind="ExternalInput").ap()
    bwt_h = nc.dram_tensor("bwt", [128, DM], F32, kind="ExternalInput").ap()
    selbc_h = nc.dram_tensor("selbc", [32, 32 * 128], F32, kind="ExternalInput").ap()
    seler_h = nc.dram_tensor("seler", [8, 128], F32, kind="ExternalInput").ap()
    ones1_h = nc.dram_tensor("ones1", [1, 128], F32, kind="ExternalInput").ap()
    selm_h = nc.dram_tensor("selm", [128, 8], F32, kind="ExternalInput").ap()
    invm_h = nc.dram_tensor("invm", [128, 8], F32, kind="ExternalInput").ap()
    out_h = nc.dram_tensor("o", [DM, TLOC], F32, kind="ExternalOutput").ap()

    kd = dict(kind="ExternalOutput") if dbg else {}
    rho_h = nc.dram_tensor("rho_scr", [DM, TLOC], F32, **kd).ap()
    yloc_h = nc.dram_tensor("yloc_scr", [DM, TLOC], F32, **kd).ap()
    if dbg:
        dbgS_h = nc.dram_tensor("dbg_S", [128, NBLK * NST], F32,
                                kind="ExternalOutput").ap()
        dbgP_h = nc.dram_tensor("dbg_P", [128, NBLK * NST], F32,
                                kind="ExternalOutput").ap()
        dbgH_h = nc.dram_tensor("dbg_H", [128, NBLK * NST], F32,
                                kind="ExternalOutput").ap()
        dbgZ_h = nc.dram_tensor("dbg_z", [DM, TLOC], F32,
                                kind="ExternalOutput").ap()
        dbgX1_h = nc.dram_tensor("dbg_x1", [DM, TLOC], F32,
                                 kind="ExternalOutput").ap()
        dbgLG_h = nc.dram_tensor("dbg_lg", [8, TLOC], F32,
                                 kind="ExternalOutput").ap()
        dbgWH_h = nc.dram_tensor("dbg_wh", [128, TLOC], BF16,
                                 kind="ExternalOutput").ap()
    cc_in = nc.dram_tensor("cc_in", [2 * DM, NST], F32)
    cc_out = nc.dram_tensor("cc_out", [2 * DM * NCORES, NST], F32)

    es = contextlib.ExitStack()
    E = es.enter_context

    class Q:
        def __init__(self, name):
            self.sem = E(nc.semaphore(name))
            self.v = 0

        def inc(self, inst, n=1):
            inst.then_inc(self.sem, n)
            self.v += n
            return self.v

    q_ld = Q("q_ld")
    q_gd = Q("q_gd")
    q_pe = Q("q_pe")
    q_act = Q("q_act")
    q_dve = Q("q_dve")
    q_cc = Q("q_cc")

    def W(eng, q, v):
        if v > 0:
            eng.wait_ge(q.sem, v)

    routeb = E(nc.sbuf_tensor("routeb", [128, NBLK, 8], BF16))
    affnb = E(nc.sbuf_tensor("affnb", [128, NBLK, 128], BF16))
    bwtb = E(nc.sbuf_tensor("bwtb", [128, NBLK, 128], BF16))
    negA = E(nc.sbuf_tensor("negA", [128, NBLK, NST], F32))
    oneD = E(nc.sbuf_tensor("oneD", [128, NBLK], F32))
    selC = E(nc.sbuf_tensor("selC_s", [32, 16, 128], F32))
    seler = E(nc.sbuf_tensor("seler_s", [8, 128], F32))
    ones1 = E(nc.sbuf_tensor("ones1_s", [1, 128], F32))
    onesk = E(nc.sbuf_tensor("onesk_s", [128, 1], F32))
    selm = E(nc.sbuf_tensor("selm_s", [128, 8], F32))
    invm = E(nc.sbuf_tensor("invm_s", [128, 8], F32))
    Hin = E(nc.sbuf_tensor("Hin", [128, NBLK, NST], F32))
    bc_all = E(nc.sbuf_tensor("bc_all", [32, NCH, TC], F32))
    C_bc = E(nc.sbuf_tensor("C_bc", [128, NST, TC], BF16))
    p_blk = E(nc.sbuf_tensor("p_blk", [128, NST, TC], F32))

    esW = contextlib.ExitStack()
    EW = esW.enter_context
    wxpb = EW(nc.sbuf_tensor("wxpb", [128, NBLK, 160], BF16))
    wdtb = EW(nc.sbuf_tensor("wdtb", [128, NBLK, 128], BF16))
    dvec_s = EW(nc.sbuf_tensor("dvec_s", [128, NBLK], F32))
    bvec = EW(nc.sbuf_tensor("bvec_s", [128, NBLK], F32))
    ones_t = EW(nc.sbuf_tensor("ones_t", [128, TC], F32))
    hcarry = EW(nc.sbuf_tensor("hcarry", [128, NBLK, NST], F32))
    rcarry = EW(nc.sbuf_tensor("rcarry", [128, NBLK], F32))
    Pc = EW(nc.sbuf_tensor("Pc", [128, NBLK, NST], F32))
    selB = EW(nc.sbuf_tensor("selB_s", [32, 16, 128], F32))

    blk = E(nc.Block())

    with nc.sbuf_tensor("wstg", [128, NBLK, 160], F32) as stg:
        def load_cast(src_ap, ncols, dst, dst_cols):
            @blk.sync
            def _(sync):
                W(sync, q_dve, q_dve.v)
                q_ld.inc(sync.dma_start(out=stg[:, :, 0:ncols], in_=src_ap), 16)

            @blk.vector
            def _(v):
                W(v, q_ld, q_ld.v)
                W(v, q_dve, q_dve.v)
                q_dve.inc(v.tensor_copy(dst[:, :, 0:dst_cols],
                                        stg[:, :, 0:ncols]))

        load_cast(wxp_h.rearrange("(blk p) m -> p blk m", p=128), 160, wxpb, 160)
        load_cast(wdt_h.rearrange("r (blk m) -> r blk m", m=128), 128, wdtb, 128)
        load_cast(affn_h.rearrange("(blk p) m -> p blk m", p=128), 128, affnb, 128)
        load_cast(bwt_h.rearrange("k (blk m) -> k blk m", m=128), 128, bwtb, 128)
        load_cast(route_h.rearrange("(blk p) m -> p blk m", p=128), 8, routeb, 8)

        @blk.sync
        def _(sync):
            W(sync, q_dve, q_dve.v)
            q_ld.inc(sync.dma_start(
                out=stg[:, :, 0:NST],
                in_=alog_h.rearrange("p (blk n) -> p blk n", n=NST)), 16)
            q_ld.inc(sync.dma_start(out=dvec_s[:], in_=dvec_h), 16)
            q_ld.inc(sync.dma_start(out=bvec[:], in_=bvec_h), 16)
            q_ld.inc(sync.dma_start(
                out=selB[:],
                in_=selbc_h[:, 0:2048].rearrange("k (j m) -> k j m", m=128)), 16)
            q_ld.inc(sync.dma_start(
                out=selC[:],
                in_=selbc_h[:, 2048:4096].rearrange("k (j m) -> k j m", m=128)), 16)
            q_ld.inc(sync.dma_start(out=seler[:], in_=seler_h), 16)
            q_ld.inc(sync.dma_start(out=ones1[:], in_=ones1_h), 16)
            q_ld.inc(sync.dma_start(out=selm[:], in_=selm_h), 16)
            q_ld.inc(sync.dma_start(out=invm[:], in_=invm_h), 16)

        @blk.scalar
        def _(s):
            W(s, q_ld, q_ld.v)
            q_act.inc(s.activation(negA[:], stg[:, :, 0:NST], AF.Exp))
            W(s, q_act, q_act.v)
            q_act.inc(s.activation(oneD[:], dvec_s[:], AF.Copy, bias=1.0))
        negA_exp = q_act.v

        @blk.vector
        def _(v):
            W(v, q_act, negA_exp)
            q_dve.inc(v.tensor_scalar(negA[:], negA[:], -1.0, None, ALU.mult))
            W(v, q_dve, q_dve.v)
            q_dve.inc(v.memset(ones_t[:], 1.0))
            q_dve.inc(v.memset(hcarry[:], 0.0))
            q_dve.inc(v.memset(rcarry[:], 0.0))
            q_dve.inc(v.memset(onesk[:], 1.0))
            W(v, q_dve, q_dve.v)
        startup_dve = q_dve.v

    bar0 = (q_ld.v, q_pe.v, q_act.v, q_dve.v)
    for sec in (blk.sync, blk.scalar, blk.vector, blk.tensor, blk.gpsimd):
        @sec
        def _(e):
            e.wait_ge(q_ld.sem, bar0[0])
            e.wait_ge(q_act.sem, bar0[2])
            e.wait_ge(q_dve.sem, bar0[3])

    esA = contextlib.ExitStack()
    EA = esA.enter_context
    NR = 3
    xf_xp = [EA(nc.sbuf_tensor(f"xf_xp{i}", [128, TC], F32)) for i in range(2)]
    xb_rot = [EA(nc.sbuf_tensor(f"xb{i}", [128, TC], BF16)) for i in range(NR)]
    xf_sc = [EA(nc.sbuf_tensor(f"xf_sc{i}", [128, TC], F32)) for i in range(2)]
    dtin_b = EA(nc.sbuf_tensor("dtin_b", [128, TC], BF16))
    sp1 = [EA(nc.sbuf_tensor("sp1_0", [128, TC], F32))]
    sp2 = [EA(nc.sbuf_tensor("sp2_0", [128, TC], F32))]
    sp3 = [EA(nc.sbuf_tensor("sp3_0", [128, TC], F32))]
    dt_rot = [EA(nc.sbuf_tensor(f"dt{i}", [128, TC], F32)) for i in range(NR)]
    dtx_rot = [EA(nc.sbuf_tensor(f"dtx{i}", [128, TC], F32)) for i in range(2)]
    rho_rot = [EA(nc.sbuf_tensor(f"rho{i}", [128, TC], F32)) for i in range(2)]
    B_bc = EA(nc.sbuf_tensor("B_bc", [128, NST, TC], F32))
    dA_s = [EA(nc.sbuf_tensor(f"dA{i}", [128, GR, TC], F32)) for i in range(2)]
    u_s = [EA(nc.sbuf_tensor("u0", [128, GR, TC], F32))] * 2
    h_s = [EA(nc.sbuf_tensor("h0", [128, GR, TC], F32))] * 2
    y_rot = [EA(nc.sbuf_tensor(f"y{i}", [128, TC], F32)) for i in range(2)]
    ps_out1 = EA(nc.psum_tensor("ps_out1", [128, TC], F32))
    ps_out2 = EA(nc.psum_tensor("ps_out2", [32, TC], F32))
    ps_dt = [EA(nc.psum_tensor(f"ps_dt{i}", [128, TC], F32)) for i in range(2)]
    ps_bb = [EA(nc.psum_tensor(f"ps_bb{i}", [128, TC], F32)) for i in range(2)]

    cast_done = [0] * NBLK
    xp_done = [0] * NBLK
    dA_all = [0] * NBLK
    dtx_made = [0] * NBLK
    dt_made = [0]
    sp_psum_rd = [0, 0]
    rho_stored = [0] * NBLK
    y_stored = [0] * NBLK
    rc_copy = [0] * NBLK
    hc_copy = [[0] * NG for _ in range(NBLK)]
    scan_par = [0, 0]
    evac_xp_prev = 0
    dtmm_prev = 0
    ubc_prev = 0

    for c in range(NCH):
        for b in range(NBLK):
            r = b % NR
            rx = b % 2

            @blk.sync
            def _(sync):
                W(sync, q_act, cast_done[(b - 2) % NBLK])
                q_ld.inc(sync.dma_start(
                    out=xf_xp[rx][:],
                    in_=x_h[b * 128:(b + 1) * 128, c * TC:(c + 1) * TC]), 16)
            ld1 = q_ld.v

            @blk.scalar
            def _(s):
                W(s, q_ld, ld1)
                W(s, q_pe, xp_done[(b - NR) % NBLK])
                q_act.inc(s.copy(xb_rot[r][:], xf_xp[rx][:]))
            cast_done[b] = q_act.v

            @blk.tensor
            def _(t):
                W(t, q_act, cast_done[b])
                if b == 0:
                    W(t, q_act, evac_xp_prev)
                t.matmul(ps_out1[:], wxpb[:, b, 0:128], xb_rot[r][:],
                         start=(b == 0), stop=(b == NBLK - 1))
                q_pe.inc(t.matmul(ps_out2[:], wxpb[:, b, 128:160], xb_rot[r][:],
                                  start=(b == 0), stop=(b == NBLK - 1)))
            xp_done[b] = q_pe.v
        xp_all = q_pe.v

        @blk.scalar
        def _(s):
            W(s, q_pe, xp_all)
            W(s, q_pe, dtmm_prev)
            q_act.inc(s.copy(dtin_b[:], ps_out1[:]))
            q_act.inc(s.copy(bc_all[:, c, :], ps_out2[:]))
        evac_xp = q_act.v
        evac_xp_prev = evac_xp

        bb_evac = [0, 0]
        for j in range(2 * NST):
            n = j % NST
            is_c = j >= NST
            pb = j % 2

            @blk.tensor
            def _(t):
                W(t, q_act, evac_xp)
                W(t, q_act, bb_evac[pb])
                if j == 0:
                    W(t, q_dve, ubc_prev)
                q_pe.inc(t.matmul(ps_bb[pb][:],
                                  selC[:, n, :] if is_c else selB[:, n, :],
                                  bc_all[:, c, :], start=True, stop=True))
            pe_j = q_pe.v

            @blk.scalar
            def _(s):
                W(s, q_pe, pe_j)
                dst = C_bc[:, n, :] if is_c else B_bc[:, n, :]
                q_act.inc(s.copy(dst, ps_bb[pb][:]))
            bb_evac[pb] = q_act.v
        bcast_done = q_act.v

        for b in range(NBLK):
            r = b % NR
            r2 = b % 2
            rx = b % 2

            @blk.sync
            def _(sync):
                W(sync, q_dve, dtx_made[(b - 2) % NBLK])
                q_ld.inc(sync.dma_start(
                    out=xf_sc[rx][:],
                    in_=x_h[b * 128:(b + 1) * 128, c * TC:(c + 1) * TC]), 16)
            ld2 = q_ld.v

            @blk.tensor
            def _(t):
                if b == 0:
                    W(t, q_act, bcast_done)
                W(t, q_act, sp_psum_rd[r2])
                q_pe.inc(t.matmul(ps_dt[r2][:], wdtb[:, b, :], dtin_b[:],
                                  start=True, stop=True))
            dtmm = q_pe.v
            if b == NBLK - 1:
                dtmm_prev = dtmm

            @blk.scalar
            def _(s):
                W(s, q_pe, dtmm)
                W(s, q_dve, dt_made[0])
                W(s, q_act, dA_all[(b - NR) % NBLK])
                a1 = s.activation(sp1[0][:], ps_dt[r2][:], AF.Abs,
                                  bias=bvec[:, b:b + 1])
                q_act.inc(a1)
                v_abs = q_act.v
                q_act.inc(s.activation(dt_rot[r][:], ps_dt[r2][:], AF.Relu,
                                       bias=bvec[:, b:b + 1]))
                sp_psum_rd[r2] = q_act.v
                W(s, q_act, v_abs)
                q_act.inc(s.activation(sp2[0][:], sp1[0][:], AF.Exp,
                                       scale=-1.0))
                W(s, q_act, q_act.v)
                q_act.inc(s.activation(sp3[0][:], sp2[0][:], AF.Ln, bias=1.0))
            sp_done = q_act.v

            @blk.vector
            def _(v):
                W(v, q_act, sp_done)
                q_dve.inc(v.tensor_tensor(dt_rot[r][:], dt_rot[r][:],
                                          sp3[0][:], ALU.add))
                dt_made[0] = q_dve.v
                W(v, q_dve, q_dve.v)
                W(v, q_gd, rho_stored[(b - 2) % NBLK])
                q_dve.inc(v.tensor_tensor_scan(
                    rho_rot[rx][:], ones_t[:], dt_rot[r][:],
                    rcarry[:, b:b + 1], ALU.mult, ALU.add))
                W(v, q_dve, q_dve.v)
                q_dve.inc(v.tensor_copy(rcarry[:, b:b + 1],
                                        rho_rot[rx][:, TC - 1:TC]))
                rc_copy[b] = q_dve.v
                W(v, q_ld, ld2)
                q_dve.inc(v.tensor_tensor(dtx_rot[rx][:], dt_rot[r][:],
                                          xf_sc[rx][:], ALU.mult))
                dtx_made[b] = q_dve.v
            dt_ready = dt_made[0]
            rho_ready = rc_copy[b]
            dtx_ready = dtx_made[b]

            @blk.gpsimd
            def _(g):
                W(g, q_dve, rho_ready)
                q_gd.inc(g.dma_start(
                    out=rho_h[b * 128:(b + 1) * 128, c * TC:(c + 1) * TC],
                    in_=rho_rot[rx][:]), 16)
            rho_stored[b] = q_gd.v

            for gi in range(NG):
                sg = gi % 2

                @blk.scalar
                def _(s):
                    W(s, q_dve, scan_par[sg])
                    W(s, q_dve, dt_ready)
                    for i in range(GR):
                        n = gi * GR + i
                        inst = s.activation(dA_s[sg][:, i, :], dt_rot[r][:],
                                            AF.Exp, scale=negA[:, b, n:n + 1])
                    q_act.inc(inst)
                dA_g = q_act.v
                if gi == NG - 1:
                    dA_all[b] = q_act.v

                @blk.vector
                def _(v):
                    W(v, q_act, dA_g)
                    W(v, q_dve, dtx_ready)
                    W(v, q_dve, scan_par[sg])
                    dtxb = dtx_rot[rx][:].unsqueeze(1).broadcast_to([128, GR, TC])
                    q_dve.inc(v.tensor_tensor(
                        u_s[sg][:], dtxb, B_bc[:, gi * GR:(gi + 1) * GR, :],
                        ALU.mult))
                    W(v, q_dve, q_dve.v)
                    W(v, q_dve, hc_copy[b][gi])
                    last = None
                    for i in range(GR):
                        n = gi * GR + i
                        last = v.tensor_tensor_scan(
                            h_s[sg][:, i, :], dA_s[sg][:, i, :],
                            u_s[sg][:, i, :], hcarry[:, b, n:n + 1],
                            ALU.mult, ALU.add)
                    q_dve.inc(last)
                    W(v, q_dve, q_dve.v)
                    q_dve.inc(v.tensor_copy(
                        hcarry[:, b, gi * GR:(gi + 1) * GR],
                        h_s[sg][:, :, TC - 1:TC].rearrange("p a b -> p (a b)")))
                    hc_copy[b][gi] = q_dve.v
                    W(v, q_dve, q_dve.v)
                    q_dve.inc(v.tensor_tensor(
                        p_blk[:, gi * GR:(gi + 1) * GR, :], h_s[sg][:],
                        C_bc[:, gi * GR:(gi + 1) * GR, :], ALU.mult))
                scan_par[sg] = q_dve.v

            @blk.vector
            def _(v):
                W(v, q_dve, scan_par[(NG - 1) % 2])
                W(v, q_gd, y_stored[(b - 2) % NBLK])
                W(v, q_dve, q_dve.v)
                q_dve.inc(v.tensor_reduce(
                    y_rot[b % 2][:], p_blk[:].rearrange("p n t -> p t n"),
                    AX.X, ALU.add))
            y_ready = q_dve.v

            @blk.gpsimd
            def _(g):
                W(g, q_dve, y_ready)
                q_gd.inc(g.dma_start(
                    out=yloc_h[b * 128:(b + 1) * 128, c * TC:(c + 1) * TC],
                    in_=y_rot[b % 2][:]), 16)
            y_stored[b] = q_gd.v
        ubc_prev = q_dve.v

    @blk.scalar
    def _(s):
        for b in range(NBLK):
            W(s, q_dve, rc_copy[b])
            q_act.inc(s.activation(Pc[:, b, :], negA[:, b, :], AF.Exp,
                                   scale=rcarry[:, b:b + 1]))
    pc_done = q_act.v
    hc_all_v = max(max(row) for row in hc_copy)

    @blk.gpsimd
    def _(g):
        W(g, q_act, pc_done)
        W(g, q_dve, hc_all_v)
        q_cc.inc(g.dma_start(
            out=cc_in[0:DM, :].rearrange("(blk p) n -> p blk n", p=128),
            in_=hcarry[:]), 16)
        q_cc.inc(g.dma_start(
            out=cc_in[DM:2 * DM, :].rearrange("(blk p) n -> p blk n", p=128),
            in_=Pc[:]), 16)
        g.wait_ge(q_cc.sem, q_cc.v)
        q_cc.inc(g.collective_compute(
            "AllGather", ALU.bypass,
            replica_groups=[list(range(NCORES))],
            ins=[cc_in.ap().opt()], outs=[cc_out.ap().opt()]), 1)
        g.wait_ge(q_cc.sem, q_cc.v)
        if dbg:
            q_gd.inc(g.dma_start(
                out=dbgS_h, in_=hcarry[:].rearrange("p a b -> p (a b)")), 16)
            q_gd.inc(g.dma_start(
                out=dbgP_h, in_=Pc[:].rearrange("p a b -> p (a b)")), 16)

    barA = (q_ld.v, q_gd.v, q_pe.v, q_act.v, q_dve.v, q_cc.v)
    for sec in (blk.sync, blk.scalar, blk.vector, blk.tensor, blk.gpsimd):
        @sec
        def _(e):
            e.wait_ge(q_ld.sem, barA[0])
            e.wait_ge(q_gd.sem, barA[1])
            e.wait_ge(q_pe.sem, barA[2])
            e.wait_ge(q_act.sem, barA[3])
            e.wait_ge(q_dve.sem, barA[4])
            e.wait_ge(q_cc.sem, barA[5])

    esA.close()
    esW.close()

    esG = contextlib.ExitStack()
    gS = esG.enter_context(nc.sbuf_tensor("gS", [128, NCORES, NBLK, NST], F32))
    gP = esG.enter_context(nc.sbuf_tensor("gP", [128, NCORES, NBLK, NST], F32))
    pprime = esG.enter_context(nc.sbuf_tensor("pprime", [128, NBLK * NST], F32))

    @blk.gpsimd
    def _(g):
        for j in range(NCORES):
            base = j * 2 * DM
            q_gd.inc(g.dma_start(
                out=gS[:, j, :, :],
                in_=cc_out[base:base + DM, :].rearrange(
                    "(blk p) n -> p blk n", p=128)), 16)
            q_gd.inc(g.dma_start(
                out=gP[:, j, :, :],
                in_=cc_out[base + DM:base + 2 * DM, :].rearrange(
                    "(blk p) n -> p blk n", p=128)), 16)
    gather_done = q_gd.v

    @blk.vector
    def _(v):
        W(v, q_gd, gather_done)
        q_dve.inc(v.memset(Hin[:], 0.0))
        Hf = Hin[:].rearrange("p a b -> p (a b)")
        for j in range(NCORES):
            gPj = gP[:, j, :, :].rearrange("p a b -> p (a b)")
            gSj = gS[:, j, :, :].rearrange("p a b -> p (a b)")
            W(v, q_dve, q_dve.v)
            q_dve.inc(v.tensor_scalar(pprime[:], gPj, selm[:, j:j + 1],
                                      invm[:, j:j + 1], ALU.mult, ALU.add))
            W(v, q_dve, q_dve.v)
            q_dve.inc(v.tensor_tensor(Hf, Hf, pprime[:], ALU.mult))
            W(v, q_dve, q_dve.v)
            q_dve.inc(v.tensor_scalar(pprime[:], gSj, selm[:, j:j + 1], None,
                                      ALU.mult))
            W(v, q_dve, q_dve.v)
            q_dve.inc(v.tensor_tensor(Hf, Hf, pprime[:], ALU.add))
        W(v, q_dve, q_dve.v)

    bar = (q_ld.v, q_gd.v, q_pe.v, q_act.v, q_dve.v, q_cc.v)
    for sec in (blk.sync, blk.scalar, blk.vector, blk.tensor, blk.gpsimd):
        @sec
        def _(e):
            e.wait_ge(q_ld.sem, bar[0])
            e.wait_ge(q_gd.sem, bar[1])
            e.wait_ge(q_pe.sem, bar[2])
            e.wait_ge(q_act.sem, bar[3])
            e.wait_ge(q_dve.sem, bar[4])
            e.wait_ge(q_cc.sem, bar[5])

    if dbg:
        @blk.gpsimd
        def _(g):
            q_gd.inc(g.dma_start(
                out=dbgH_h, in_=Hin[:].rearrange("p a b -> p (a b)")), 16)

    esG.close()

    esB = contextlib.ExitStack()
    EB = esB.enter_context
    NRB = 2
    rho_b = [EB(nc.sbuf_tensor(f"rhoB{i}", [128, TC], F32)) for i in range(NRB)]
    yl_b = [EB(nc.sbuf_tensor(f"ylB{i}", [128, TC], F32)) for i in range(NRB)]
    x_b = [EB(nc.sbuf_tensor(f"xB{i}", [128, TC], F32)) for i in range(NRB)]
    lam_s = [EB(nc.sbuf_tensor(f"lam{i}", [128, GR, TC], F32)) for i in range(2)]
    corr_s = [EB(nc.sbuf_tensor("corr0", [128, TC], F32))] * 2
    zS = EB(nc.sbuf_tensor("zS", [128, NBLK, TC], F32))
    sq_rot = [EB(nc.sbuf_tensor(f"sq{i}", [128, TC], F32)) for i in range(2)]
    x1b_rot = [EB(nc.sbuf_tensor(f"x1b{i}", [128, TC], BF16)) for i in range(NRB)]
    stat_s = EB(nc.sbuf_tensor("stat_s", [1, 5, TC], F32))
    rbmr = EB(nc.sbuf_tensor("rbmr", [128, 2, TC], F32))
    lg_s = EB(nc.sbuf_tensor("lg_s", [8, TC], F32))
    tmp4 = EB(nc.sbuf_tensor("tmp4B", [4, TC], F32))
    m4 = EB(nc.sbuf_tensor("m4B", [4, TC], F32))
    rw8 = EB(nc.sbuf_tensor("rw8", [8, TC], F32))
    hf_s = EB(nc.sbuf_tensor("hf_s", [128, TC], F32))
    wh_s = EB(nc.sbuf_tensor("wh_s", [128, TC], BF16))
    out_rot = [EB(nc.sbuf_tensor(f"outB{i}", [128, TC], F32)) for i in range(NRB)]
    ps_lg = EB(nc.psum_tensor("ps_lg", [8, TC], F32))
    ps_hf = EB(nc.psum_tensor("ps_hf", [128, TC], F32))
    ps_cs = [EB(nc.psum_tensor(f"ps_cs{i}", [128, TC], F32)) for i in range(2)]
    ps_m8 = EB(nc.psum_tensor("ps_m8", [8, TC], F32))
    ps_s1 = EB(nc.psum_tensor("ps_s1", [1, TC], F32))
    ps_s2 = EB(nc.psum_tensor("ps_s2", [1, TC], F32))

    z_made = [0] * NBLK
    zdump = [0] * NBLK
    x1dump = [0] * NBLK
    lam_all = [0] * NBLK
    corrp_par = [0, 0]
    corr_free = [0, 0]
    sq_used = [0, 0]
    sq2_used = [0, 0]
    x1b_used = [0] * NBLK
    z2_made = [0, 0]
    out_stored = [0] * NBLK
    stats_rd_prev = 0
    csel_guard = 0
    lgps_guard = 0

    for c in range(NCH):
        cs_evac = [0, 0]
        for n in range(NST):
            pb = n % 2

            @blk.tensor
            def _(t):
                W(t, q_act, cs_evac[pb])
                if n < 2:
                    W(t, q_dve, csel_guard)
                q_pe.inc(t.matmul(ps_cs[pb][:], selC[:, n, :],
                                  bc_all[:, c, :], start=True, stop=True))
            pe_n = q_pe.v

            @blk.scalar
            def _(s):
                W(s, q_pe, pe_n)
                q_act.inc(s.copy(C_bc[:, n, :], ps_cs[pb][:]))
            cs_evac[pb] = q_act.v
        cbc_done = q_act.v

        for b in range(NBLK):
            r = b % NRB

            @blk.sync
            def _(sync):
                W(sync, q_dve, z_made[(b - NRB) % NBLK])
                W(sync, q_act, lam_all[(b - NRB) % NBLK])
                q_ld.inc(sync.dma_start(
                    out=rho_b[r][:],
                    in_=rho_h[b * 128:(b + 1) * 128, c * TC:(c + 1) * TC]), 16)
                q_ld.inc(sync.dma_start(
                    out=yl_b[r][:],
                    in_=yloc_h[b * 128:(b + 1) * 128, c * TC:(c + 1) * TC]), 16)
                q_ld.inc(sync.dma_start(
                    out=x_b[r][:],
                    in_=x_h[b * 128:(b + 1) * 128, c * TC:(c + 1) * TC]), 16)
            ld_b = q_ld.v

            for gi in range(NG):
                sg = gi % 2

                @blk.scalar
                def _(s):
                    W(s, q_ld, ld_b)
                    W(s, q_dve, corrp_par[sg])
                    for i in range(GR):
                        n = gi * GR + i
                        inst = s.activation(lam_s[sg][:, i, :], rho_b[r][:],
                                            AF.Exp, scale=negA[:, b, n:n + 1])
                    q_act.inc(inst)
                lam_g = q_act.v
                if gi == NG - 1:
                    lam_all[b] = q_act.v

                @blk.vector
                def _(v):
                    W(v, q_act, lam_g)
                    W(v, q_act, cbc_done)
                    for i in range(GR):
                        n = gi * GR + i
                        W(v, q_dve, q_dve.v)
                        q_dve.inc(v.scalar_tensor_tensor(
                            p_blk[:, n, :], lam_s[sg][:, i, :],
                            Hin[:, b, n:n + 1], C_bc[:, n, :],
                            ALU.mult, ALU.mult))
                corrp_par[sg] = q_dve.v

            @blk.vector
            def _(v):
                rc = b % 2
                W(v, q_dve, corrp_par[(NG - 1) % 2])
                W(v, q_ld, ld_b)
                W(v, q_dve, corr_free[rc])
                W(v, q_dve, q_dve.v)
                q_dve.inc(v.tensor_reduce(
                    corr_s[rc][:], p_blk[:].rearrange("p n t -> p t n"),
                    AX.X, ALU.add))
                W(v, q_dve, q_dve.v)
                q_dve.inc(v.tensor_tensor(yl_b[r][:], yl_b[r][:],
                                          corr_s[rc][:], ALU.add))
                corr_free[rc] = q_dve.v
                W(v, q_dve, q_dve.v)
                q_dve.inc(v.scalar_tensor_tensor(
                    zS[:, b, :], x_b[r][:], oneD[:, b:b + 1], yl_b[r][:],
                    ALU.mult, ALU.add))
            z_made[b] = q_dve.v
            z_here = q_dve.v
            if dbg:
                @blk.gpsimd
                def _(g):
                    W(g, q_dve, z_here)
                    q_gd.inc(g.dma_start(
                        out=dbgZ_h[b * 128:(b + 1) * 128,
                                   c * TC:(c + 1) * TC],
                        in_=zS[:, b, :]), 16)
                zdump[b] = q_gd.v

            @blk.scalar
            def _(s):
                W(s, q_dve, z_here)
                W(s, q_pe, sq_used[b % 2])
                q_act.inc(s.activation(sq_rot[b % 2][:], zS[:, b, :], AF.Square))
            sq_here = q_act.v

            @blk.tensor
            def _(t):
                W(t, q_act, sq_here)
                if b == 0:
                    W(t, q_dve, stats_rd_prev)
                t.matmul(ps_s1[:], onesk[:], zS[:, b, :],
                         start=(b == 0), stop=(b == NBLK - 1))
                q_pe.inc(t.matmul(ps_s2[:], onesk[:], sq_rot[b % 2][:],
                                  start=(b == 0), stop=(b == NBLK - 1)))
            sq_used[b % 2] = q_pe.v
        stats1_pe = q_pe.v

        def stats_math(stats_pe):
            @blk.vector
            def _(v):
                W(v, q_pe, stats_pe)
                W(v, q_dve, q_dve.v)
                q_dve.inc(v.tensor_scalar(stat_s[:, 0, :], ps_s1[:], 1.0 / DM,
                                          None, ALU.mult))
                W(v, q_dve, q_dve.v)
                q_dve.inc(v.tensor_scalar(stat_s[:, 1, :], ps_s2[:], 1.0 / DM,
                                          None, ALU.mult))
                W(v, q_dve, q_dve.v)
                q_dve.inc(v.tensor_tensor(stat_s[:, 2, :], stat_s[:, 0, :],
                                          stat_s[:, 0, :], ALU.mult))
                W(v, q_dve, q_dve.v)
                q_dve.inc(v.tensor_tensor(stat_s[:, 2, :], stat_s[:, 1, :],
                                          stat_s[:, 2, :], ALU.subtract))
            var_v = q_dve.v

            @blk.vector
            def _(v):
                W(v, q_dve, var_v)
                q_dve.inc(v.tensor_scalar(stat_s[:, 2, :], stat_s[:, 2, :], EPS,
                                          None, ALU.add))
            vare_v = q_dve.v

            @blk.scalar
            def _(s):
                W(s, q_dve, vare_v)
                q_act.inc(s.activation(stat_s[:, 3, :], stat_s[:, 2, :], AF.Ln))
                W(s, q_act, q_act.v)
                q_act.inc(s.activation(stat_s[:, 3, :], stat_s[:, 3, :], AF.Exp,
                                       scale=-0.5))
            rstd_v = q_act.v

            @blk.vector
            def _(v):
                W(v, q_act, rstd_v)
                W(v, q_dve, q_dve.v)
                q_dve.inc(v.tensor_tensor(stat_s[:, 4, :], stat_s[:, 0, :],
                                          stat_s[:, 3, :], ALU.mult))
            return q_dve.v

        mr1 = stats_math(stats1_pe)
        stats_rd1 = mr1

        @blk.tensor
        def _(t):
            W(t, q_dve, mr1)
            W(t, q_act, cs_evac[0])
            W(t, q_act, cs_evac[1])
            t.matmul(ps_cs[0][:], ones1[:], stat_s[:, 3, :], start=True,
                     stop=True)
            q_pe.inc(t.matmul(ps_cs[1][:], ones1[:], stat_s[:, 4, :],
                              start=True, stop=True))
        bc1_pe = q_pe.v

        @blk.scalar
        def _(s):
            W(s, q_pe, bc1_pe)
            s.copy(rbmr[:, 0, :], ps_cs[0][:])
            q_act.inc(s.copy(rbmr[:, 1, :], ps_cs[1][:]))
        bc1 = q_act.v

        for b in range(NBLK):
            @blk.vector
            def _(v):
                W(v, q_act, bc1)
                if dbg:
                    W(v, q_gd, zdump[b])
                W(v, q_dve, q_dve.v)
                q_dve.inc(v.tensor_tensor(zS[:, b, :], zS[:, b, :],
                                          rbmr[:, 0, :], ALU.mult))
                W(v, q_dve, q_dve.v)
                q_dve.inc(v.tensor_tensor(zS[:, b, :], zS[:, b, :],
                                          rbmr[:, 1, :], ALU.subtract))
            x1_here = q_dve.v
            if dbg:
                @blk.gpsimd
                def _(g):
                    W(g, q_dve, x1_here)
                    q_gd.inc(g.dma_start(
                        out=dbgX1_h[b * 128:(b + 1) * 128,
                                    c * TC:(c + 1) * TC],
                        in_=zS[:, b, :]), 16)
                x1dump[b] = q_gd.v

            @blk.scalar
            def _(s):
                W(s, q_dve, x1_here)
                W(s, q_pe, x1b_used[(b - NRB) % NBLK])
                q_act.inc(s.copy(x1b_rot[b % NRB][:], zS[:, b, :]))
            x1b_here = q_act.v

            @blk.tensor
            def _(t):
                W(t, q_act, x1b_here)
                if b == 0:
                    W(t, q_act, lgps_guard)
                t.matmul(ps_lg[:], routeb[:, b, :], x1b_rot[b % NRB][:],
                         start=(b == 0), stop=(b == NBLK - 1))
                q_pe.inc(t.matmul(ps_hf[:], affnb[:, b, :], x1b_rot[b % NRB][:],
                                  start=(b == 0), stop=(b == NBLK - 1)))
            x1b_used[b] = q_pe.v
        ffn_in_pe = q_pe.v

        @blk.scalar
        def _(s):
            W(s, q_pe, ffn_in_pe)
            s.copy(lg_s[:], ps_lg[:])
            q_act.inc(s.copy(hf_s[:], ps_hf[:]))
        lg_evac = q_act.v
        lgps_guard = lg_evac

        @blk.gpsimd
        def _(g):
            W(g, q_act, lg_evac)
            q_gd.inc(g.dma_start(out=tmp4[:], in_=lg_s[4:8, :]), 16)
        t4a = q_gd.v

        @blk.vector
        def _(v):
            W(v, q_gd, t4a)
            W(v, q_dve, q_dve.v)
            q_dve.inc(v.tensor_tensor(m4[:], lg_s[0:4, :], tmp4[:], ALU.max))
        m4d = q_dve.v

        @blk.gpsimd
        def _(g):
            W(g, q_dve, m4d)
            q_gd.inc(g.dma_start(out=tmp4[0:2, :], in_=m4[2:4, :]), 16)
        t4b = q_gd.v

        @blk.vector
        def _(v):
            W(v, q_gd, t4b)
            W(v, q_dve, q_dve.v)
            q_dve.inc(v.tensor_tensor(m4[0:2, :], m4[0:2, :], tmp4[0:2, :], ALU.max))
        m2d = q_dve.v

        @blk.gpsimd
        def _(g):
            W(g, q_dve, m2d)
            q_gd.inc(g.dma_start(out=tmp4[0:1, :], in_=m4[1:2, :]), 16)
        t4c = q_gd.v

        @blk.vector
        def _(v):
            W(v, q_gd, t4c)
            W(v, q_dve, q_dve.v)
            q_dve.inc(v.tensor_tensor(m4[0:1, :], m4[0:1, :], tmp4[0:1, :], ALU.max))
        m1d = q_dve.v

        @blk.tensor
        def _(t):
            W(t, q_dve, m1d)
            q_pe.inc(t.matmul(ps_m8[:], ones1[:, 0:8], m4[0:1, :], start=True,
                              stop=True))
        m8pe = q_pe.v

        @blk.vector
        def _(v):
            W(v, q_pe, m8pe)
            W(v, q_dve, q_dve.v)
            q_dve.inc(v.tensor_tensor(rw8[:], lg_s[:], ps_m8[:], ALU.is_equal))
            W(v, q_dve, q_dve.v)
            q_dve.inc(v.tensor_tensor(rw8[:], rw8[:], lg_s[:], ALU.mult))
        rw8d = q_dve.v

        @blk.tensor
        def _(t):
            W(t, q_dve, rw8d)
            q_pe.inc(t.matmul(ps_hf[:], seler[:], rw8[:], start=True, stop=True))
        mask_pe = q_pe.v

        @blk.vector
        def _(v):
            W(v, q_pe, mask_pe)
            W(v, q_dve, q_dve.v)
            q_dve.inc(v.tensor_tensor(wh_s[:], hf_s[:], ps_hf[:], ALU.mult))
        wh_made = q_dve.v
        if dbg:
            @blk.gpsimd
            def _(g):
                W(g, q_dve, wh_made)
                q_gd.inc(g.dma_start(
                    out=dbgLG_h[:, c * TC:(c + 1) * TC], in_=lg_s[:]), 16)
                q_gd.inc(g.dma_start(
                    out=dbgWH_h[:, c * TC:(c + 1) * TC], in_=wh_s[:]), 16)

        for b in range(NBLK):
            pb = b % 2

            @blk.tensor
            def _(t):
                W(t, q_dve, wh_made)
                W(t, q_dve, z2_made[pb])
                q_pe.inc(t.matmul(ps_cs[pb][:], bwtb[:, b, :], wh_s[:],
                                  start=True, stop=True))
            fo_pe = q_pe.v

            @blk.vector
            def _(v):
                W(v, q_pe, fo_pe)
                if dbg:
                    W(v, q_gd, x1dump[b])
                W(v, q_dve, q_dve.v)
                q_dve.inc(v.tensor_tensor(zS[:, b, :], zS[:, b, :],
                                          ps_cs[pb][:], ALU.add))
            z2_made[pb] = q_dve.v
            z2_here = q_dve.v

            @blk.scalar
            def _(s):
                W(s, q_dve, z2_here)
                W(s, q_pe, sq2_used[b % 2])
                q_act.inc(s.activation(sq_rot[b % 2][:], zS[:, b, :], AF.Square))
            sq2_here = q_act.v

            @blk.tensor
            def _(t):
                W(t, q_act, sq2_here)
                if b == 0:
                    W(t, q_dve, stats_rd1)
                t.matmul(ps_s1[:], onesk[:], zS[:, b, :],
                         start=(b == 0), stop=(b == NBLK - 1))
                q_pe.inc(t.matmul(ps_s2[:], onesk[:], sq_rot[b % 2][:],
                                  start=(b == 0), stop=(b == NBLK - 1)))
            sq2_used[b % 2] = q_pe.v
        stats2_pe = q_pe.v

        mr2 = stats_math(stats2_pe)
        stats_rd_prev = mr2

        @blk.tensor
        def _(t):
            W(t, q_dve, mr2)
            q_pe.inc(t.matmul(ps_cs[0][:], ones1[:], stat_s[:, 3, :],
                              start=True, stop=True))
            q_pe.inc(t.matmul(ps_cs[1][:], ones1[:], stat_s[:, 4, :],
                              start=True, stop=True))
        bc2_pe = q_pe.v

        @blk.scalar
        def _(s):
            W(s, q_pe, bc2_pe)
            s.copy(rbmr[:, 0, :], ps_cs[0][:])
            q_act.inc(s.copy(rbmr[:, 1, :], ps_cs[1][:]))
        bc2 = q_act.v

        for b in range(NBLK):
            @blk.vector
            def _(v):
                W(v, q_act, bc2)
                W(v, q_gd, out_stored[(b - NRB) % NBLK])
                W(v, q_dve, q_dve.v)
                q_dve.inc(v.tensor_tensor(out_rot[b % NRB][:], zS[:, b, :],
                                          rbmr[:, 0, :], ALU.mult))
                W(v, q_dve, q_dve.v)
                q_dve.inc(v.tensor_tensor(out_rot[b % NRB][:],
                                          out_rot[b % NRB][:], rbmr[:, 1, :],
                                          ALU.subtract))
            o_here = q_dve.v

            @blk.gpsimd
            def _(g):
                W(g, q_dve, o_here)
                q_gd.inc(g.dma_start(
                    out=out_h[b * 128:(b + 1) * 128, c * TC:(c + 1) * TC],
                    in_=out_rot[b % NRB][:]), 16)
            out_stored[b] = q_gd.v

        csel_guard = q_dve.v

    @blk.gpsimd
    def _(g):
        g.wait_ge(q_gd.sem, q_gd.v)

    esB.close()
    es.close()
    return nc


def kernel(L0_attn_x, L0_attn_A_log, L0_attn_D, L0_attn_x_proj_w,
           L0_attn_dt_proj_w, L0_attn_dt_proj_b, L0_ffn_route_w, L0_ffn_A,
           L0_ffn_Bw):
    x = np.asarray(L0_attn_x, dtype=np.float32)
    A_log = np.asarray(L0_attn_A_log, dtype=np.float32)
    Dv = np.asarray(L0_attn_D, dtype=np.float32)
    xp_w = np.asarray(L0_attn_x_proj_w, dtype=np.float32)
    dt_w = np.asarray(L0_attn_dt_proj_w, dtype=np.float32)
    dt_b = np.asarray(L0_attn_dt_proj_b, dtype=np.float32)
    route_w = np.asarray(L0_ffn_route_w, dtype=np.float32)
    A_ffn = np.asarray(L0_ffn_A, dtype=np.float32)
    Bw = np.asarray(L0_ffn_Bw, dtype=np.float32)

    wxp = np.ascontiguousarray(xp_w.T)
    wdt = np.ascontiguousarray(dt_w.T)
    alog = np.ascontiguousarray(
        A_log.reshape(NBLK, 128, NST).transpose(1, 0, 2)).reshape(
        128, NBLK * NST)
    dvec = np.ascontiguousarray(Dv.reshape(NBLK, 128).T)
    bvec = np.ascontiguousarray(dt_b.reshape(NBLK, 128).T)
    route = np.ascontiguousarray(route_w.T)
    affn = np.ascontiguousarray(np.transpose(A_ffn, (2, 0, 1)).reshape(DM, 128))
    bwt = np.ascontiguousarray(np.transpose(Bw, (0, 2, 1)).reshape(128, DM))
    selbc = np.zeros((32, 32 * 128), dtype=np.float32)
    for j in range(32):
        selbc[j, j * 128:(j + 1) * 128] = 1.0
    seler = np.zeros((8, 128), dtype=np.float32)
    for e in range(8):
        seler[e, e * 16:(e + 1) * 16] = 1.0
    ones1 = np.ones((1, 128), dtype=np.float32)

    nc = build_nc()
    in_maps = []
    for cid in range(NCORES):
        bb, qq = cid // 4, cid % 4
        xc = np.ascontiguousarray(x[bb, qq * TLOC:(qq + 1) * TLOC, :].T)
        sel = np.zeros(8, dtype=np.float32)
        for j in range(NCORES):
            if j // 4 == bb and j % 4 < qq:
                sel[j] = 1.0
        in_maps.append({
            "x": xc, "wxp": wxp, "wdt": wdt, "alog": alog, "dvec": dvec,
            "bvec": bvec, "route": route, "affn": affn, "bwt": bwt,
            "selbc": selbc, "seler": seler, "ones1": ones1,
            "selm": np.ascontiguousarray(np.tile(sel, (128, 1))),
            "invm": np.ascontiguousarray(np.tile(1.0 - sel, (128, 1))),
        })
    res = run_bass_kernel_spmd(nc, in_maps, core_ids=list(range(NCORES)))
    out = np.empty((2, 8192, DM), dtype=np.float32)
    for cid in range(NCORES):
        bb, qq = cid // 4, cid % 4
        out[bb, qq * TLOC:(qq + 1) * TLOC, :] = res.results[cid]["o"].T
    return out
